# version 06
# baseline (speedup 1.0000x reference)
import numpy as np
from contextlib import ExitStack

import concourse.bass as bass
import concourse.mybir as mybir
from concourse.bass_utils import run_bass_kernel_spmd

F32 = mybir.dt.float32
BF16 = mybir.dt.bfloat16
I32 = mybir.dt.int32
U8 = mybir.dt.uint8
AF = mybir.ActivationFunctionType
ALU = mybir.AluOpType
AXX = mybir.AxisListType.X

ENGS = ("sync", "gpsimd", "tensor", "vector", "scalar")

NSEQ = 2
D = 1024
NDC = 8
T = 2048
ST = 1024
NST = 2
NTT = 2
H = 4096
NHC = 16
EPS_LN = 1e-5
EPS_AFF = 1e-6


class Bld:
    def __init__(self, nc):
        self.nc = nc
        self.seg_i = 0
        self.q = {e: [] for e in ENGS}
        self.cnt = {}
        self.sems = {}
        self.hold = {}

    def seg(self):
        self.seg_i += 1

    def add_sem(self, name, handle):
        self.sems[name] = handle
        self.cnt[name] = 0

    def do(self, eng, fn, waits=(), inc=None, n=1):
        waits = tuple(w for w in waits if w is not None)
        self.q[eng].append((self.seg_i, fn, waits, inc, n))
        if inc is None:
            return None
        self.cnt[inc] += n
        return (inc, self.cnt[inc])

    def dma(self, eng, sem, out, in_, waits=()):
        return self.do(eng, lambda E, o=out, i=in_: E.dma_start(out=o, in_=i),
                       waits, inc=sem, n=16)

    def emit(self, block):
        final = dict(self.cnt)
        for eng in ENGS:
            for _, fn, waits, inc, n in self.q[eng]:
                for s, v in waits:
                    assert v <= final[s], f"{eng}: wait {s}>={v} > final {final[s]}"
        sems = self.sems

        def mk(eng, ops, wm):
            def body(E):
                for _, fn, waits, inc, n in ops:
                    for s, v in waits:
                        if wm.get(s, 0) < v:
                            E.wait_ge(sems[s], v)
                            wm[s] = v
                    ins = fn(E)
                    if inc is not None:
                        ins.then_inc(sems[inc], n)
            return body

        for si in range(self.seg_i + 1):
            for eng in ENGS:
                ops = [op for op in self.q[eng] if op[0] == si]
                if ops:
                    wm = self.__dict__.setdefault("_wm", {}).setdefault(eng, {})
                    getattr(block, eng)(mk(eng, ops, wm))


GATE_CUT = [None]


def build_program(mode="full"):
    GATE_CUT[0] = {"AG1": 1, "AG2": 2, "AG3": 3}.get(mode)
    nc = bass.Bass("TRN2", target_bir_lowering=False,
                   detect_race_conditions=False)

    dx = nc.dram_tensor("x", [NSEQ, D, T], F32, kind="ExternalInput").ap()
    daw = nc.dram_tensor("aw", [D, 2 * D], F32, kind="ExternalInput").ap()
    dab = nc.dram_tensor("ab", [128, 16], F32, kind="ExternalInput").ap()
    dgam = nc.dram_tensor("gam", [128, 8], F32, kind="ExternalInput").ap()
    dbet = nc.dram_tensor("bet", [128, 8], F32, kind="ExternalInput").ap()
    dwg = nc.dram_tensor("wg", [D, 8], F32, kind="ExternalInput").ap()
    dbg = nc.dram_tensor("bg", [1, 8], F32, kind="ExternalInput").ap()
    dw1m = nc.dram_tensor("w1m", [D, H], F32, kind="ExternalInput").ap()
    dw2m = nc.dram_tensor("w2m", [H, D], F32, kind="ExternalInput").ap()
    db1m = nc.dram_tensor("b1m", [128, 32], F32, kind="ExternalInput").ap()
    db2m = nc.dram_tensor("b2m", [128, 8], F32, kind="ExternalInput").ap()
    dw1e = nc.dram_tensor("w1e", [8, D, H], F32, kind="ExternalInput").ap()
    dw2e = nc.dram_tensor("w2e", [8, H, D], F32, kind="ExternalInput").ap()
    db1t = nc.dram_tensor("b1t", [8, 128, 32], F32, kind="ExternalInput").ap()
    db2t = nc.dram_tensor("b2t", [8, 128, 8], F32, kind="ExternalInput").ap()
    dal = nc.dram_tensor("al", [1, 1], F32, kind="ExternalInput").ap()
    dio = nc.dram_tensor("io8", [1, 8], F32, kind="ExternalInput").ap()
    dsel = nc.dram_tensor("sel", [4, 512], F32, kind="ExternalInput").ap()
    dout = nc.dram_tensor("out", [NSEQ, D, T], F32, kind="ExternalOutput").ap()
    dx1 = nc.dram_tensor("x1s", [NSEQ, D, T], F32, kind="Internal").ap()
    dln2 = nc.dram_tensor("ln2s", [NSEQ, D, T], BF16, kind="Internal").ap()
    dws1 = nc.dram_tensor("wsel1", [2, D, H], F32, kind="Internal").ap()
    dws2 = nc.dram_tensor("wsel2", [2, H, D], F32, kind="Internal").ap()
    dfen = nc.dram_tensor("dfence", [128, 8], F32, kind="Internal").ap()

    ctx = ExitStack()
    with ctx:
        def sbt(name, shp, dt):
            return ctx.enter_context(nc.sbuf_tensor(name, shp, dt))

        W1A = sbt("W1A", [128, 16384], BF16)
        W2A = sbt("W2A", [128, 16384], BF16)
        F1 = sbt("F1", [128, 8192], F32)
        F2 = sbt("F2", [128, 8192], BF16)
        F3 = sbt("F3", [128, 16384], BF16)
        F4 = sbt("F4", [128, 9216], F32)
        F5 = sbt("F5", [128, 6144], F32)
        glt = sbt("glt", [1, 64], F32)
        grow = sbt("grow", [1, 8], F32)
        idxi = sbt("idxi", [1, 2], I32)
        msk = sbt("msk", [128, 1024], U8)
        onesc = sbt("onesc", [128, 1], F32)
        onesb = sbt("onesb", [128, 1], BF16)
        onesrow = sbt("onesrow", [1, 128], F32)
        abT = sbt("abT", [128, 16], F32)
        gamT = sbt("gamT", [128, 8], F32)
        betT = sbt("betT", [128, 8], F32)
        b1mT = sbt("b1mT", [128, 32], F32)
        b2mT = sbt("b2mT", [128, 8], F32)
        wgT = sbt("wgT", [128, 64], F32)
        bgT = sbt("bgT", [1, 8], F32)
        io8T = sbt("io8T", [1, 8], F32)
        alT = sbt("alT", [1, 1], F32)
        yStb = sbt("yStb", [128, 32], BF16)
        y2Stb = sbt("y2Stb", [128, 16], BF16)
        sgb = sbt("sgb", [128, 8], F32)
        abc = sbt("abc", [128, 2], F32)
        wkbc = sbt("wkbc", [128, 2], F32)
        B2t = sbt("B2t", [128, 8], F32)
        b1g = sbt("b1g", [128, 64], F32)
        b2g = sbt("b2g", [128, 16], F32)
        carry = sbt("carry", [128, 8], F32)
        macc = sbt("macc", [128, 8], F32)
        tmp8 = sbt("tmp8", [128, 8], F32)
        epsb = sbt("epsb", [128, 2], F32)

        pb = [ctx.enter_context(nc.psum_tensor(f"pb{i}", [128, 512], F32))
              for i in range(8)]
        rg0 = ctx.enter_context(nc.gpsimd.register("rg0"))
        rg1 = ctx.enter_context(nc.gpsimd.register("rg1"))

        b = Bld(nc)
        for s in ("sxL", "sxS", "sxR", "sxB", "gW1", "gW2", "gB", "gS",
                  "tv", "vv", "sv"):
            b.add_sem(s, ctx.enter_context(nc.semaphore(s)))

        xsf = W2A[:, :].bitcast(F32)
        F3f = F3[:, :].bitcast(F32)
        HH = F3f[:, 0:1024]
        KK = F3f[:, 1024:2048]
        CC = F3f[:, 2048:3072]
        GG = F3f[:, 3072:4096]
        HB = F4[:, 0:1024]
        T_ = [F4[:, 1024 + i * 512:1024 + (i + 1) * 512] for i in range(4)]
        T12 = F4[:, 1024:2048]
        XR = [F4[:, 3072 + i * 1024:3072 + (i + 1) * 1024] for i in range(2)]
        BCA = F5[:, 0:1024]
        BCB = F5[:, 1024:2048]
        BCP = F5[:, 2048:3072]
        BCQ = F5[:, 3072:4096]
        BCR = F5[:, 4096:5120]
        BCM = F5[:, 5120:6144]
        sbst = F4[0:4, 5120:7168]
        selT = F4[0:4, 7168:7680]

        def xs(dc, c0=0, c1=ST):
            return xsf[:, dc * ST + c0:dc * ST + c1]

        def Y(dc, c0=0, c1=ST):
            return F1[:, dc * ST + c0:dc * ST + c1]

        OA = Y

        def F2c(kc, c0=0, c1=ST):
            return F2[:, kc * ST + c0:kc * ST + c1]

        def F3b(dc, c0=0, c1=ST):
            return F3[:, dc * ST + c0:dc * ST + c1]

        def Tt(hc, c0=0, c1=ST):
            return F3[:, hc * ST + c0:hc * ST + c1]

        def Wa(kc, c0, c1):
            return W1A[:, kc * 2048 + c0:kc * 2048 + c1]

        W1h = Wa

        def W2h(hc, c0, c1):
            return W2A[:, hc * 1024 + c0:hc * 1024 + c1]

        bankW = [None] * 8
        bankR = [None] * 8

        def fence(eng, sem, waits=()):
            return b.do(eng, lambda E: E.dma_start(
                out=dfen[:, 0:1], in_=onesc[:, 0:1],
                allow_slow_non_contiguous=True), waits, inc=sem, n=16)

        def mm(psl, lhsT, rhs, start, stop, waits=(), bank=None):
            w = list(waits)
            if start and bank is not None and bankR[bank] is not None:
                w.append(bankR[bank])
            t = b.do("tensor",
                     lambda E, o=psl, l=lhsT, r=rhs, st=start, sp=stop:
                     E.matmul(o, l, r, start=st, stop=sp),
                     w, inc="tv" if stop else None)
            if stop and bank is not None:
                bankW[bank] = t
            return t

        def rb(eng, fn, bank, waits=(), sem=None):
            w = list(waits)
            if bankW[bank] is not None:
                w.append(bankW[bank])
            t = b.do(eng, fn, w, inc=sem or ("vv" if eng == "vector" else "sv"))
            bankR[bank] = t
            return t

        def vop(fn, waits=(), inc=None):
            return b.do("vector", fn, waits, inc=inc)

        def sop(fn, waits=(), inc=None):
            return b.do("scalar", fn, waits, inc=inc)

        for dst, src in [(abT, dab), (gamT, dgam), (betT, dbet), (b1mT, db1m),
                         (b2mT, db2m), (bgT, dbg), (io8T, dio), (alT, dal)]:
            b.dma("sync", "sxB", dst[:, :], src[:, :])
        for dc in range(NDC):
            b.dma("sync", "sxB", wgT[:, dc * 8:(dc + 1) * 8],
                  dwg[dc * 128:(dc + 1) * 128, :])
        b.dma("sync", "sxB", selT, dsel[:, :])
        cw = [fence("sync", "sxB")]

        lastc = [None]

        def vc(fn, waits=()):
            lastc[0] = vop(fn, list(waits) + [lastc[0]], inc="vv")
            return lastc[0]

        vop(lambda E: E.memset(onesc[:, :], 1.0))
        vop(lambda E: E.memset(onesb[:, :], 1.0))
        vop(lambda E: E.memset(onesrow[:, :], 1.0))
        vop(lambda E: E.memset(epsb[:, 0:1], EPS_LN))
        vop(lambda E: E.memset(epsb[:, 1:2], EPS_AFF))
        vop(lambda E: E.memset(sbst[:, :], 0.0))
        vop(lambda E: E.memset(yStb[:, :], 0.0))
        vop(lambda E: E.memset(y2Stb[:, :], 0.0))
        vc(lambda E: E.tensor_mul(out=tmp8[:, :], in0=gamT[:, :],
                                  in1=gamT[:, :]), cw)
        for dc in range(NDC):
            vc(lambda E, dc=dc: E.memset(yStb[:, dc * 4:dc * 4 + 1], 1.0))
            vc(lambda E, dc=dc: E.memset(y2Stb[:, dc * 2:dc * 2 + 1], 1.0))
            vc(lambda E, dc=dc: E.tensor_copy(out=yStb[:, dc * 4 + 1:dc * 4 + 2],
                                              in_=gamT[:, dc:dc + 1]))
            vc(lambda E, dc=dc: E.tensor_copy(out=yStb[:, dc * 4 + 2:dc * 4 + 3],
                                              in_=tmp8[:, dc:dc + 1]))
            vc(lambda E, dc=dc: E.tensor_copy(out=y2Stb[:, dc * 2 + 1:dc * 2 + 2],
                                              in_=tmp8[:, dc:dc + 1]))
        vc(lambda E: E.tensor_reduce(out=T_[0][:, 1:2], in_=tmp8[:, :],
                                     axis=AXX, op=ALU.add))
        vc(lambda E: E.tensor_mul(out=tmp8[:, :], in0=gamT[:, :], in1=betT[:, :]))
        for dc in range(NDC):
            vc(lambda E, dc=dc: E.tensor_copy(out=yStb[:, dc * 4 + 3:dc * 4 + 4],
                                              in_=tmp8[:, dc:dc + 1]))
        vc(lambda E: E.tensor_reduce(out=T_[0][:, 2:3], in_=tmp8[:, :],
                                     axis=AXX, op=ALU.add))
        vc(lambda E: E.tensor_mul(out=tmp8[:, :], in0=betT[:, :], in1=betT[:, :]))
        vc(lambda E: E.tensor_reduce(out=T_[0][:, 4:5], in_=tmp8[:, :],
                                     axis=AXX, op=ALU.add))
        vc(lambda E: E.tensor_reduce(out=T_[0][:, 0:1], in_=gamT[:, :],
                                     axis=AXX, op=ALU.add))
        vc(lambda E: E.tensor_reduce(out=T_[0][:, 3:4], in_=betT[:, :],
                                     axis=AXX, op=ALU.add))
        t0 = vc(lambda E: E.engine_nop())
        for i in range(5):
            mm(pb[0][0:1, i:i + 1], T_[0][:, i:i + 1], onesc[:, :],
               True, True, [t0], bank=0)
        lastc[0] = rb("vector", lambda E: E.tensor_copy(out=glt[0:1, 8:13],
                                             in_=pb[0][0:1, 0:5]), 0,
                      waits=[lastc[0]])
        vc(lambda E: E.tensor_scalar(out=glt[0:1, 11:12], in0=glt[0:1, 11:12],
                                     scalar1=1.0 / D, scalar2=None, op0=ALU.mult))
        for dc in range(NDC):
            mm(pb[1][0:1, 0:8], onesc[:, :], wgT[:, dc * 8:(dc + 1) * 8],
               dc == 0, dc == NDC - 1, cw, bank=1)
        rb("vector", lambda E: E.tensor_copy(out=glt[0:1, 16:24],
                                             in_=pb[1][0:1, 0:8]), 1)
        vc(lambda E: E.tensor_scalar(out=glt[0:1, 0:1], in0=alT[:, :],
                                     scalar1=0.1, scalar2=1.0,
                                     op0=ALU.max, op1=ALU.min), cw)
        vc(lambda E: E.tensor_scalar(out=glt[0:1, 1:2], in0=glt[0:1, 0:1],
                                     scalar1=-1.0, scalar2=1.0,
                                     op0=ALU.mult, op1=ALU.add))
        ta = vc(lambda E: E.engine_nop())
        mm(pb[2][:, 0:2], onesrow[0:1, :], glt[0:1, 0:2], True, True, [ta], bank=2)
        rb("vector", lambda E: E.tensor_copy(out=abc[:, :], in_=pb[2][:, 0:2]), 2)
        tg = vop(lambda E: E.engine_nop(), inc="vv")
        mm(pb[3][:, 0:5], onesrow[0:1, :], glt[0:1, 8:13], True, True, [tg], bank=3)
        rb("vector", lambda E: E.tensor_copy(out=sgb[:, 0:5], in_=pb[3][:, 0:5]), 3)
        t_consts = vop(lambda E: E.engine_nop(), inc="vv")

        tA_end = {}
        tA_sqe = {}
        tA_projend = {}
        tA_ystmm = {}
        tA_l2mm = {}
        tB_mm1 = {}
        tB_mm2 = {}
        tB5v = {}
        tB5s = {}
        tB5mm = {}
        tOA = {}
        tx1w = {}
        tln2w = {}
        toutw = {}
        tgmm = {}
        twa = {}

        def phase_a_stripe(s, p):
            c0 = p * ST
            first = (s, p) == (0, 0)
            prev = None if first else ((s, p - 1) if p == 1 else (s - 1, 1))

            xw = [t_consts]
            if not first:
                if p == 1:
                    xw += [tA_end[prev], tA_sqe[prev]]
                else:
                    xw += [tB5v.get((s - 1, 1)), tB5s.get((s - 1, 1)),
                           tB_mm2.get((s - 1, 1, 2, 1)), toutw.get(s - 1)]
            for dc in range(NDC):
                b.dma("sync", "sxL", xs(dc),
                      dx[s, dc * 128:(dc + 1) * 128, c0:c0 + ST],
                      xw if dc == 0 else ())
            t_xf = fence("sync", "sxL")
            t_xld = [t_xf] * NDC

            wF2 = []
            wF3 = []
            if not first:
                if p == 1:
                    wF2 = [tln2w[prev]]
                    wF3 = [tA_l2mm[prev]]
                else:
                    wF2 = [tB5mm.get((s - 1, 1))]
                    wF3 = [tB5mm.get((s - 1, 1))]
            t_xbf, t_xsq = [], []
            for dc in range(NDC):
                t_xbf.append(vop(lambda E, dc=dc: E.tensor_copy(
                    out=F2c(dc), in_=xs(dc)),
                    [t_xld[dc]] + (wF2 if dc == 0 else []), inc="vv"))
                t_xsq.append(sop(lambda E, dc=dc: E.activation(
                    out=F3b(dc), in_=xs(dc), func=AF.Square),
                    [t_xld[dc]] + (wF3 if dc == 0 else []), inc="sv"))
            for dc in range(NDC):
                for tt in range(NTT):
                    mm(pb[tt][0:1, :], onesb[:, :], F2c(dc, tt * 512, (tt + 1) * 512),
                       dc == 0, dc == NDC - 1, [t_xbf[dc]], bank=tt)
                    mm(pb[2 + tt][0:1, :], onesb[:, :], F3b(dc, tt * 512, (tt + 1) * 512),
                       dc == 0, dc == NDC - 1, [t_xsq[dc]], bank=2 + tt)
            for tt in range(NTT):
                rb("vector", lambda E, tt=tt: E.tensor_copy(
                    out=sbst[0:1, tt * 512:(tt + 1) * 512], in_=pb[tt][0:1, :]), tt)
                rb("vector", lambda E, tt=tt: E.tensor_copy(
                    out=sbst[0:1, 1024 + tt * 512:1024 + (tt + 1) * 512],
                    in_=pb[2 + tt][0:1, :]), 2 + tt)
            t_stev = vop(lambda E: E.engine_nop(), inc="vv")

            for tt in range(NTT):
                mm(pb[4 + tt][:, :], selT[0:4, 0:128],
                   sbst[0:4, tt * 512:(tt + 1) * 512], True, True, [t_stev],
                   bank=4 + tt)
                mm(pb[6 + tt][:, :], selT[0:4, 0:128],
                   sbst[0:4, 1024 + tt * 512:1024 + (tt + 1) * 512], True, True,
                   [t_stev], bank=6 + tt)
            for tt in range(NTT):
                rb("vector", lambda E, tt=tt: E.tensor_scalar(
                    out=BCA[:, tt * 512:(tt + 1) * 512], in0=pb[4 + tt][:, :],
                    scalar1=1.0 / D, scalar2=None, op0=ALU.mult), 4 + tt,
                   waits=[("sv", b.cnt["sv"])] if tt == 0 else ())
            vop(lambda E: E.tensor_mul(out=T12[:, :], in0=BCA[:, :], in1=BCA[:, :]))
            for tt in range(NTT):
                rb("vector", lambda E, tt=tt: E.scalar_tensor_tensor(
                    out=BCB[:, tt * 512:(tt + 1) * 512], in0=pb[6 + tt][:, :],
                    scalar=1.0 / D, in1=T12[:, tt * 512:(tt + 1) * 512],
                    op0=ALU.mult, op1=ALU.subtract), 6 + tt)
            tvv = ("vv", b.cnt["vv"])
            sop(lambda E: E.activation(out=BCB[:, :], in_=BCB[:, :],
                                       func=AF.Ln, bias=epsb[:, 0:1]), [tvv])
            t_r1 = sop(lambda E: E.activation(out=BCB[:, :], in_=BCB[:, :],
                                              func=AF.Exp, scale=-0.5), inc="sv")

            t_lnx = []
            for dc in range(NDC):
                w = [t_r1] if dc == 0 else []
                if p == 0 and s > 0 and dc == 0:
                    w.append(toutw.get(s - 1))
                vop(lambda E, dc=dc: E.tensor_sub(
                    out=Y(dc), in0=xs(dc), in1=BCA), w)
                t_lnx.append(vop(lambda E, dc=dc: E.tensor_mul(
                    out=F2c(dc), in0=Y(dc), in1=BCB), inc="vv"))

            if p == 0:
                waw = list(cw)
                if s > 0:
                    waw.append(tB_mm1.get((s - 1, 1, 2, 1)))
                for kc in range(NDC):
                    b.dma("gpsimd", "gW1", Wa(kc, 0, 2048),
                          daw[kc * 128:(kc + 1) * 128, :],
                          waw if kc == 0 else ())
                twa[s] = [fence("gpsimd", "gW1")] * NDC

            t_y = [None] * NDC
            t_scan = [None] * NDC
            for dc in range(NDC):
                q = dc % 2
                bk = [4 * q + i for i in range(4)]
                for kc in range(NDC):
                    for mi, mc in enumerate((dc, dc + 8)):
                        for tt in range(NTT):
                            mm(pb[bk[2 * mi + tt]][:, :],
                               Wa(kc, mc * 128, (mc + 1) * 128),
                               F2c(kc, tt * 512, (tt + 1) * 512),
                               kc == 0, kc == NDC - 1,
                               [t_lnx[kc], twa[s][kc]],
                               bank=bk[2 * mi + tt])
                for mi, dst in ((0, HH), (1, KK)):
                    mc = dc + 8 * mi
                    for tt in range(NTT):
                        rb("scalar", lambda E, d=dst, tt=tt, mc=mc, q=q, mi=mi:
                           E.activation(out=d[:, tt * 512:(tt + 1) * 512],
                                        in_=pb[4 * q + 2 * mi + tt][:, :],
                                        func=AF.Identity,
                                        bias=abT[:, mc:mc + 1]),
                           bk[2 * mi + tt],
                           waits=[t_scan[dc - 1]] if (mi == 0 and tt == 0
                                                      and dc > 0) else ())
                t_c = sop(lambda E: E.activation(out=CC[:, :], in_=KK[:, :],
                                                 func=AF.Sigmoid, scale=-1.0),
                          inc="sv")
                t_s1 = sop(lambda E: E.activation(out=GG[:, :], in_=HH[:, :],
                                                  func=AF.Sigmoid), inc="sv")
                vop(lambda E: E.tensor_scalar(out=msk[:, :], in0=HH[:, :],
                                              scalar1=0.0, scalar2=None,
                                              op0=ALU.is_ge), [t_c])
                vop(lambda E: E.tensor_scalar(out=KK[:, :], in0=HH[:, :],
                                              scalar1=0.5, scalar2=None,
                                              op0=ALU.add))
                vop(lambda E: E.select(out=HH[:, :], mask=msk[:, :],
                                       on_true=KK[:, :], on_false=GG[:, :]),
                    [t_s1])
                vop(lambda E: E.tensor_mul(out=KK[:, :], in0=HH[:, :],
                                           in1=CC[:, :]))
                vop(lambda E: E.tensor_sub(out=KK[:, :], in0=HH[:, :],
                                           in1=KK[:, :]))
                tfix = None
                if p == 1:
                    f1 = vop(lambda E, dc=dc: E.tensor_scalar(
                        out=T_[0][:, 0:1], in0=CC[:, 0:1],
                        scalar1=carry[:, dc:dc + 1], scalar2=None,
                        op0=ALU.mult), inc="vv")
                    f2 = vop(lambda E: E.tensor_add(out=T_[0][:, 1:2],
                                                    in0=KK[:, 0:1],
                                                    in1=T_[0][:, 0:1]), [f1],
                             inc="vv")
                    tfix = vop(lambda E: E.tensor_copy(out=KK[:, 0:1],
                                                       in_=T_[0][:, 1:2]), [f2],
                               inc="vv")
                t_scan[dc] = vop(lambda E: E.tensor_tensor_scan(
                    out=HB[:, :], data0=CC[:, :], data1=KK[:, :], initial=0.0,
                    op0=ALU.mult, op1=ALU.add), [tfix], inc="vv")
                if p == 0:
                    vop(lambda E, dc=dc: E.tensor_copy(
                        out=carry[:, dc:dc + 1], in_=HB[:, 1023:1024]),
                        [t_scan[dc]])
                vop(lambda E, dc=dc: E.tensor_mul(out=Y(dc), in0=Y(dc), in1=BCB))
                t_y[dc] = vop(lambda E, dc=dc: E.tensor_add(
                    out=Y(dc), in0=Y(dc), in1=HB[:, :]), [t_scan[dc]], inc="vv")
            t_proj = b.do("tensor", lambda E: E.nop(), inc="tv")
            tA_projend[(s, p)] = t_proj

            t_ybf, t_ysq = [], []
            for dc in range(NDC):
                t_ybf.append(vop(lambda E, dc=dc: E.tensor_copy(
                    out=F2c(dc), in_=Y(dc)),
                    [t_y[dc]] + ([t_proj] if dc == 0 else []), inc="vv"))
                t_ysq.append(sop(lambda E, dc=dc: E.activation(
                    out=F3b(dc), in_=Y(dc), func=AF.Square),
                    [t_y[dc]] + ([t_proj, t_y[NDC - 1]] if dc == 0 else []),
                    inc="sv"))
            for dc in range(NDC):
                for tt in range(NTT):
                    mm(pb[tt][0:4, :], yStb[:, dc * 4:(dc + 1) * 4],
                       F2c(dc, tt * 512, (tt + 1) * 512),
                       dc == 0, dc == NDC - 1, [t_ybf[dc]], bank=tt)
                    mm(pb[2 + tt][0:2, :], y2Stb[:, dc * 2:(dc + 1) * 2],
                       F3b(dc, tt * 512, (tt + 1) * 512),
                       dc == 0, dc == NDC - 1, [t_ysq[dc]], bank=2 + tt)
            tA_ystmm[(s, p)] = ("tv", b.cnt["tv"])
            for tt in range(NTT):
                rb("vector", lambda E, tt=tt: E.tensor_copy(
                    out=sbst[0:4, tt * 512:(tt + 1) * 512], in_=pb[tt][0:4, :]), tt)
                rb("vector", lambda E, tt=tt: E.tensor_copy(
                    out=sbst[0:2, 1024 + tt * 512:1024 + (tt + 1) * 512],
                    in_=pb[2 + tt][0:2, :]), 2 + tt)
            t_yst = vop(lambda E: E.engine_nop(), inc="vv")

            rows = [(0, 0), (1, 0), (2, 0), (3, 0), (0, 1024), (1, 1024)]
            for tt in range(NTT):
                sl = slice(tt * 512, (tt + 1) * 512)
                for i, (r, off) in enumerate(rows):
                    mm(pb[i][:, :], selT[0:4, r * 128:(r + 1) * 128],
                       sbst[0:4, off + tt * 512:off + tt * 512 + 512],
                       True, True, [t_yst], bank=i)

                def V(fn, banks=(), waits=()):
                    w = list(waits)
                    for k in banks:
                        if bankW[k] is not None:
                            w.append(bankW[k])
                    t = b.do("vector", fn, w, inc="vv")
                    for k in banks:
                        bankR[k] = t
                    return t

                V(lambda E, sl=sl: E.tensor_scalar(
                    out=BCA[:, sl], in0=pb[0][:, :], scalar1=1.0 / D,
                    scalar2=None, op0=ALU.mult), banks=[0])
                V(lambda E, sl=sl: E.tensor_mul(out=T_[1][:, :], in0=BCA[:, sl],
                                                in1=BCA[:, sl]))
                V(lambda E: E.scalar_tensor_tensor(
                    out=T_[0][:, :], in0=pb[4][:, :], scalar=1.0 / D,
                    in1=T_[1][:, :], op0=ALU.mult, op1=ALU.subtract), banks=[4])
                tv1 = ("vv", b.cnt["vv"])
                sop(lambda E: E.activation(out=T_[0][:, :], in_=T_[0][:, :],
                                           func=AF.Ln, bias=epsb[:, 1:2]), [tv1])
                t_r2 = sop(lambda E, sl=sl: E.activation(
                    out=BCB[:, sl], in_=T_[0][:, :], func=AF.Exp, scale=-0.5),
                    inc="sv")
                V(lambda E, sl=sl: E.tensor_scalar(
                    out=T_[0][:, :], in0=BCA[:, sl], scalar1=sgb[:, 0:1],
                    scalar2=None, op0=ALU.mult), waits=[t_r2])
                V(lambda E: E.tensor_sub(out=T_[0][:, :], in0=pb[1][:, :],
                                         in1=T_[0][:, :]), banks=[1])
                V(lambda E, sl=sl: E.tensor_mul(out=T_[0][:, :], in0=T_[0][:, :],
                                                in1=BCB[:, sl]))
                V(lambda E: E.tensor_scalar(
                    out=T_[1][:, :], in0=T_[0][:, :], scalar1=1.0 / D,
                    scalar2=sgb[:, 3:4], op0=ALU.mult, op1=ALU.add))
                V(lambda E: E.tensor_scalar(
                    out=T_[2][:, :], in0=pb[2][:, :], scalar1=2.0,
                    scalar2=None, op0=ALU.mult), banks=[2])
                V(lambda E, sl=sl: E.tensor_scalar(
                    out=T_[3][:, :], in0=BCA[:, sl], scalar1=sgb[:, 1:2],
                    scalar2=None, op0=ALU.mult))
                V(lambda E: E.tensor_sub(out=T_[2][:, :], in0=T_[2][:, :],
                                         in1=T_[3][:, :]))
                V(lambda E, sl=sl: E.tensor_mul(out=T_[2][:, :], in0=T_[2][:, :],
                                                in1=BCA[:, sl]))
                V(lambda E: E.tensor_sub(out=T_[2][:, :], in0=pb[5][:, :],
                                         in1=T_[2][:, :]), banks=[5])
                V(lambda E, sl=sl: E.tensor_mul(out=T_[3][:, :], in0=BCB[:, sl],
                                                in1=BCB[:, sl]))
                V(lambda E: E.tensor_mul(out=T_[2][:, :], in0=T_[2][:, :],
                                         in1=T_[3][:, :]))
                V(lambda E, sl=sl: E.tensor_scalar(
                    out=T_[3][:, :], in0=BCA[:, sl], scalar1=sgb[:, 2:3],
                    scalar2=None, op0=ALU.mult))
                V(lambda E: E.tensor_sub(out=T_[3][:, :], in0=pb[3][:, :],
                                         in1=T_[3][:, :]), banks=[3])
                V(lambda E, sl=sl: E.tensor_mul(out=T_[3][:, :], in0=T_[3][:, :],
                                                in1=BCB[:, sl]))
                V(lambda E: E.scalar_tensor_tensor(
                    out=T_[2][:, :], in0=T_[3][:, :], scalar=2.0,
                    in1=T_[2][:, :], op0=ALU.mult, op1=ALU.add))
                V(lambda E: E.tensor_scalar(
                    out=T_[2][:, :], in0=T_[2][:, :], scalar1=sgb[:, 4:5],
                    scalar2=1.0 / D, op0=ALU.add, op1=ALU.mult))
                V(lambda E: E.tensor_mul(out=T_[3][:, :], in0=T_[1][:, :],
                                         in1=T_[1][:, :]))
                V(lambda E: E.tensor_sub(out=T_[2][:, :], in0=T_[2][:, :],
                                         in1=T_[3][:, :]))
                tv2 = ("vv", b.cnt["vv"])
                sop(lambda E: E.activation(out=T_[2][:, :], in_=T_[2][:, :],
                                           func=AF.Ln, bias=epsb[:, 0:1]), [tv2])
                t_r3 = sop(lambda E, sl=sl: E.activation(
                    out=BCR[:, sl], in_=T_[2][:, :], func=AF.Exp, scale=-0.5),
                    inc="sv")
                V(lambda E, sl=sl: E.tensor_mul(out=BCP[:, sl], in0=BCB[:, sl],
                                                in1=BCR[:, sl]), waits=[t_r3])
                V(lambda E, sl=sl: E.tensor_mul(out=BCQ[:, sl], in0=BCP[:, sl],
                                                in1=BCA[:, sl]))
                V(lambda E, sl=sl: E.tensor_mul(out=BCM[:, sl], in0=T_[1][:, :],
                                                in1=BCR[:, sl]))

            t_ap = [None] * NDC
            for dc in range(NDC):
                vop(lambda E, dc=dc: E.tensor_mul(out=HB[:, :], in0=Y(dc),
                                                  in1=BCP))
                vop(lambda E: E.tensor_sub(out=HB[:, :], in0=HB[:, :],
                                           in1=BCQ))
                vop(lambda E, dc=dc: E.tensor_scalar(
                    out=HB[:, :], in0=HB[:, :], scalar1=gamT[:, dc:dc + 1],
                    scalar2=None, op0=ALU.mult))
                vop(lambda E, dc=dc: E.scalar_tensor_tensor(
                    out=T12[:, :], in0=BCR, scalar=betT[:, dc:dc + 1],
                    in1=BCM, op0=ALU.mult, op1=ALU.subtract))
                vop(lambda E, dc=dc: E.tensor_add(out=xs(dc), in0=xs(dc),
                                                  in1=HB[:, :]))
                t_ap[dc] = vop(lambda E, dc=dc: E.tensor_add(
                    out=xs(dc), in0=xs(dc), in1=T12[:, :]), inc="vv")
                tx1w.setdefault((s, p), None)
                tx1w[(s, p)] = b.dma("sync", "sxS",
                                     dx1[s, dc * 128:(dc + 1) * 128, c0:c0 + ST],
                                     xs(dc), [t_ap[dc]])

            wyst = [tA_ystmm[(s, p)]]
            t_1bf, t_1sq = [], []
            for dc in range(NDC):
                t_1bf.append(vop(lambda E, dc=dc: E.tensor_copy(
                    out=F2c(dc), in_=xs(dc)),
                    [t_ap[dc]] + (wyst if dc == 0 else []), inc="vv"))
                t_1sq.append(sop(lambda E, dc=dc: E.activation(
                    out=F3b(dc), in_=xs(dc), func=AF.Square),
                    [t_ap[dc]] + (wyst if dc == 0 else []), inc="sv"))
            for dc in range(NDC):
                for tt in range(NTT):
                    mm(pb[tt][0:1, :], onesb[:, :], F2c(dc, tt * 512, (tt + 1) * 512),
                       dc == 0, dc == NDC - 1, [t_1bf[dc]], bank=tt)
                    mm(pb[2 + tt][0:1, :], onesb[:, :], F3b(dc, tt * 512, (tt + 1) * 512),
                       dc == 0, dc == NDC - 1, [t_1sq[dc]], bank=2 + tt)
            tA_l2mm[(s, p)] = ("tv", b.cnt["tv"])
            for tt in range(NTT):
                rb("vector", lambda E, tt=tt: E.tensor_copy(
                    out=sbst[0:1, tt * 512:(tt + 1) * 512], in_=pb[tt][0:1, :]), tt)
                rb("vector", lambda E, tt=tt: E.tensor_copy(
                    out=sbst[0:1, 1024 + tt * 512:1024 + (tt + 1) * 512],
                    in_=pb[2 + tt][0:1, :]), 2 + tt)
            t_st2 = vop(lambda E: E.engine_nop(), inc="vv")
            for tt in range(NTT):
                mm(pb[4 + tt][:, :], selT[0:4, 0:128],
                   sbst[0:4, tt * 512:(tt + 1) * 512], True, True, [t_st2],
                   bank=4 + tt)
                mm(pb[6 + tt][:, :], selT[0:4, 0:128],
                   sbst[0:4, 1024 + tt * 512:1024 + (tt + 1) * 512], True, True,
                   [t_st2], bank=6 + tt)
            for tt in range(NTT):
                rb("vector", lambda E, tt=tt: E.tensor_scalar(
                    out=BCA[:, tt * 512:(tt + 1) * 512], in0=pb[4 + tt][:, :],
                    scalar1=1.0 / D, scalar2=None, op0=ALU.mult), 4 + tt,
                   waits=[("sv", b.cnt["sv"])] if tt == 0 else ())
            vop(lambda E: E.tensor_mul(out=T12[:, :], in0=BCA[:, :], in1=BCA[:, :]))
            for tt in range(NTT):
                rb("vector", lambda E, tt=tt: E.scalar_tensor_tensor(
                    out=BCB[:, tt * 512:(tt + 1) * 512], in0=pb[6 + tt][:, :],
                    scalar=1.0 / D, in1=T12[:, tt * 512:(tt + 1) * 512],
                    op0=ALU.mult, op1=ALU.subtract), 6 + tt)
            tv3 = ("vv", b.cnt["vv"])
            sop(lambda E: E.activation(out=BCB[:, :], in_=BCB[:, :],
                                       func=AF.Ln, bias=epsb[:, 0:1]), [tv3])
            t_r4 = sop(lambda E: E.activation(out=BCB[:, :], in_=BCB[:, :],
                                              func=AF.Exp, scale=-0.5), inc="sv")
            tA_sqe[(s, p)] = t_r4

            if p == 0:
                wm0 = [tgmm.get(s - 1)] if s > 0 else []
                vop(lambda E: E.memset(macc[:, :], 0.0), wm0)
            wln = [("tv", b.cnt["tv"]), t_r4]
            for dc in range(NDC):
                vop(lambda E, dc=dc: E.tensor_sub(out=HB[:, :], in0=xs(dc),
                                                  in1=BCA),
                    wln if dc == 0 else [])
                vop(lambda E: E.tensor_mul(out=HB[:, :], in0=HB[:, :], in1=BCB))
                t_l2 = vop(lambda E, dc=dc: E.tensor_copy(out=F2c(dc),
                                                          in_=HB[:, :]),
                           inc="vv")
                tred = vop(lambda E: E.tensor_reduce(
                    out=T_[0][:, 0:1], in_=HB[:, :], axis=AXX, op=ALU.add),
                    inc="vv")
                vop(lambda E, dc=dc: E.tensor_add(out=macc[:, dc:dc + 1],
                                                  in0=macc[:, dc:dc + 1],
                                                  in1=T_[0][:, 0:1]), [tred])
                tln2w[(s, p)] = b.dma("sync", "sxS",
                                      dln2[s, dc * 128:(dc + 1) * 128, c0:c0 + ST],
                                      F2c(dc), [t_l2])
            tA_end[(s, p)] = vop(lambda E: E.engine_nop(), inc="vv")

        def gate(s):
            last = [None]

            def vq(fn, waits=()):
                last[0] = vop(fn, list(waits) + [last[0]], inc="vv")
                return last[0]

            w0 = [tA_end[(s, 1)]]
            t_ms = vq(lambda E: E.tensor_scalar(
                out=macc[:, :], in0=macc[:, :], scalar1=1.0 / T, scalar2=None,
                op0=ALU.mult), w0)
            for dc in range(NDC):
                tgmm[s] = mm(pb[0][0:1, 0:8], macc[:, dc:dc + 1],
                             wgT[:, dc * 8:(dc + 1) * 8], dc == 0, dc == NDC - 1,
                             [t_ms], bank=0)
            last[0] = rb("vector", lambda E: E.tensor_copy(
                out=grow[:, :], in_=pb[0][0:1, 0:8]), 0, waits=[last[0]])
            vq(lambda E: E.tensor_add(out=grow[:, :], in0=grow[:, :],
                                      in1=bgT[:, :]))
            vq(lambda E: E.tensor_reduce(out=glt[0:1, 32:33], in_=grow[:, :],
                                         axis=AXX, op=ALU.max))
            t_m = vq(lambda E: E.tensor_scalar(out=glt[0:1, 33:34],
                                               in0=glt[0:1, 32:33],
                                               scalar1=-1.0, scalar2=None,
                                               op0=ALU.mult))
            t_e = sop(lambda E: E.activation(out=glt[0:1, 24:32], in_=grow[:, :],
                                             func=AF.Exp, bias=glt[0:1, 33:34]),
                      [t_m], inc="sv")
            vq(lambda E: E.tensor_reduce(out=glt[0:1, 34:35], in_=glt[0:1, 24:32],
                                         axis=AXX, op=ALU.add), [t_e])
            vq(lambda E: E.reciprocal(out=glt[0:1, 35:36], in_=glt[0:1, 34:35]))
            vq(lambda E: E.tensor_scalar(out=grow[:, :], in0=glt[0:1, 24:32],
                                         scalar1=glt[0:1, 35:36], scalar2=None,
                                         op0=ALU.mult))
            vq(lambda E: E.tensor_reduce(out=glt[0:1, 36:37], in_=grow[:, :],
                                         axis=AXX, op=ALU.max))
            vq(lambda E: E.tensor_scalar(out=glt[0:1, 24:32], in0=grow[:, :],
                                         scalar1=glt[0:1, 36:37], scalar2=None,
                                         op0=ALU.is_ge))
            vq(lambda E: E.tensor_mul(out=glt[0:1, 48:56], in0=grow[:, :],
                                      in1=glt[0:1, 24:32]))
            vq(lambda E: E.tensor_sub(out=glt[0:1, 48:56], in0=grow[:, :],
                                      in1=glt[0:1, 48:56]))
            vq(lambda E: E.tensor_reduce(out=glt[0:1, 37:38], in_=glt[0:1, 48:56],
                                         axis=AXX, op=ALU.max))
            vq(lambda E: E.tensor_scalar(out=glt[0:1, 56:64], in0=glt[0:1, 48:56],
                                         scalar1=glt[0:1, 37:38], scalar2=None,
                                         op0=ALU.is_ge))
            vq(lambda E: E.tensor_mul(out=glt[0:1, 48:56], in0=glt[0:1, 24:32],
                                      in1=io8T[:, :]))
            vq(lambda E: E.tensor_reduce(out=glt[0:1, 44:45], in_=glt[0:1, 48:56],
                                         axis=AXX, op=ALU.add))
            vq(lambda E: E.tensor_mul(out=glt[0:1, 56:64], in0=glt[0:1, 56:64],
                                      in1=io8T[:, :]))
            vq(lambda E: E.tensor_reduce(out=glt[0:1, 45:46], in_=glt[0:1, 56:64],
                                         axis=AXX, op=ALU.add))
            vq(lambda E: E.tensor_add(out=glt[0:1, 38:39], in0=glt[0:1, 36:37],
                                      in1=glt[0:1, 37:38]))
            vq(lambda E: E.reciprocal(out=glt[0:1, 39:40], in_=glt[0:1, 38:39]))
            vq(lambda E: E.tensor_scalar(out=glt[0:1, 40:41], in0=glt[0:1, 36:37],
                                         scalar1=glt[0:1, 39:40], scalar2=None,
                                         op0=ALU.mult))
            vq(lambda E: E.tensor_scalar(out=glt[0:1, 41:42], in0=glt[0:1, 37:38],
                                         scalar1=glt[0:1, 39:40], scalar2=None,
                                         op0=ALU.mult))
            vq(lambda E: E.tensor_scalar(out=glt[0:1, 42:43], in0=glt[0:1, 40:41],
                                         scalar1=glt[0:1, 1:2], scalar2=None,
                                         op0=ALU.mult))
            vq(lambda E: E.tensor_scalar(out=glt[0:1, 43:44], in0=glt[0:1, 41:42],
                                         scalar1=glt[0:1, 1:2], scalar2=None,
                                         op0=ALU.mult))
            vq(lambda E: E.tensor_copy(out=idxi[0:1, 0:1], in_=glt[0:1, 44:45]),
               [("gB", b.cnt["gB"])])
            t_idx = vq(lambda E: E.tensor_copy(out=idxi[0:1, 1:2],
                                               in_=glt[0:1, 45:46]))
            mm(pb[1][:, 0:2], onesrow[0:1, :], glt[0:1, 42:44], True, True,
               [t_idx], bank=1)
            t_wk = rb("vector", lambda E: E.tensor_copy(out=wkbc[:, :],
                                                        in_=pb[1][:, 0:2]), 1)
            def ld0(E):
                ins = E.reg_load(rg0, idxi[0:1, 0:1])
                b.hold[("v", s, 0)] = E.snap(rg0, min_val=0, max_val=7)
                return ins

            def ld1(E):
                ins = E.reg_load(rg1, idxi[0:1, 1:2])
                b.hold[("v", s, 1)] = E.snap(rg1, min_val=0, max_val=7)
                return ins

            if GATE_CUT[0] == 1:
                b.dma("sync", "sxS", dout[0, 0:1, 0:64], glt[0:1, :], [t_wk])
                b.dma("sync", "sxS", dout[0, 1:2, 0:8], grow[0:1, :], [t_wk])
                return t_wk
            wB = [t_idx]
            if s > 0:
                wB += [tB_mm2.get((s - 1, 1, 2, 1)), ("sv", b.cnt["sv"]),
                       ("vv", b.cnt["vv"]),
                       ("gW1", b.cnt["gW1"]), ("gW2", b.cnt["gW2"]),
                       ("gS", b.cnt["gS"])]
            b.do("gpsimd", ld0, wB)
            b.do("gpsimd", ld1, ())
            for dst, src, k in ((b1g[:, 0:32], db1t, 0), (b1g[:, 32:64], db1t, 1),
                                (b2g[:, 0:8], db2t, 0), (b2g[:, 8:16], db2t, 1)):
                b.do("gpsimd",
                     lambda E, d=dst, sr=src, k=k: E.dma_start(
                         out=d, in_=sr[b.hold[("v", s, k)]]),
                     (), inc="gB", n=16)
            tgB = fence("gpsimd", "gB")
            if GATE_CUT[0] == 2:
                return tgB
            for k in range(2):
                b.do("gpsimd", lambda E, k=k: E.dma_start(
                    out=dws1[k], in_=dw1e[b.hold[("v", s, k)]]),
                    (), inc="gS", n=16)
                b.do("gpsimd", lambda E, k=k: E.dma_start(
                    out=dws2[k], in_=dw2e[b.hold[("v", s, k)]]),
                    (), inc="gS", n=16)
            tgS = fence("gpsimd", "gS")
            for k in range(2):
                b.hold[("g1", s, k)] = tgS
                b.hold[("g2", s, k)] = tgS
            if GATE_CUT[0] == 3:
                return tgS
            tb0 = vop(lambda E: E.tensor_scalar(out=B2t[:, :], in0=b2mT[:, :],
                                          scalar1=abc[:, 0:1], scalar2=None,
                                          op0=ALU.mult),
                [tgB, t_wk, ("sv", b.cnt["sv"])], inc="vv")
            tb1 = vop(lambda E: E.scalar_tensor_tensor(
                out=B2t[:, :], in0=b2g[:, 0:8], scalar=wkbc[:, 0:1],
                in1=B2t[:, :], op0=ALU.mult, op1=ALU.add), [tb0], inc="vv")
            t_b2 = vop(lambda E: E.scalar_tensor_tensor(
                out=B2t[:, :], in0=b2g[:, 8:16], scalar=wkbc[:, 1:2],
                in1=B2t[:, :], op0=ALU.mult, op1=ALU.add), [tb1], inc="vv")
            return t_b2

        def phase_b(s, t_b2):
            passes = [(p, j, h) for p in range(NST) for j in range(3)
                      for h in range(2)]
            for pi, (p, j, h) in enumerate(passes):
                c0 = p * ST
                if j == 0 and h == 0:
                    wl = []
                    if p == 0:
                        pass
                    else:
                        wl = [tB_mm1[(s, 0, 2, 1)], tB5mm[(s, 0)],
                              tB5v[(s, 0)]]
                    for kc in range(NDC):
                        b.dma("sync", "sxR", F2c(kc),
                              dln2[s, kc * 128:(kc + 1) * 128, c0:c0 + ST],
                              wl if kc == 0 else ())
                    b.hold[("ln", s, p)] = [fence("sync", "sxR")] * NDC
                tln = b.hold[("ln", s, p)]

                if pi == 0:
                    wW1 = [tA_projend[(s, 1)]]
                else:
                    wW1 = [tB_mm1[(s,) + passes[pi - 1]]]
                if j > 0:
                    wW1.append(b.hold[("g1", s, j - 1)])
                w1src = dw1m if j == 0 else dws1[j - 1]
                for kc in range(NDC):
                    b.dma("gpsimd", "gW1", W1h(kc, 0, 2048),
                          w1src[kc * 128:(kc + 1) * 128,
                                h * 2048:(h + 1) * 2048],
                          wW1 if kc == 0 else ())
                tW1 = [fence("gpsimd", "gW1")] * NDC

                t_ev1 = [None] * NHC
                for hc in range(NHC):
                    bk = [(hc % 2) * 2 + tt for tt in range(NTT)]
                    for kc in range(NDC):
                        for tt in range(NTT):
                            mm(pb[bk[tt]][:, :], W1h(kc, hc * 128, (hc + 1) * 128),
                               F2c(kc, tt * 512, (tt + 1) * 512),
                               kc == 0, kc == NDC - 1,
                               [tW1[kc], tln[kc]], bank=bk[tt])
                    if j == 0:
                        bia = b1mT[:, h * 16 + hc:h * 16 + hc + 1]
                    else:
                        bia = b1g[:, (j - 1) * 32 + h * 16 + hc:
                                  (j - 1) * 32 + h * 16 + hc + 1]
                    for tt in range(NTT):
                        t_ev1[hc] = rb(
                            "scalar",
                            lambda E, hc=hc, tt=tt, bia=bia: E.activation(
                                out=Tt(hc, tt * 512, (tt + 1) * 512),
                                in_=pb[(hc % 2) * 2 + tt][:, :],
                                func=AF.Gelu_apprx_tanh, bias=bia),
                            bk[tt],
                            waits=[t_b2] if (pi == 0 and hc == 0) else ())
                tB_mm1[(s, p, j, h)] = ("tv", b.cnt["tv"])

                if pi == 0:
                    wW2 = [tB5v.get((s - 1, 1)), tB5s.get((s - 1, 1)),
                           tx1w[(s, 0)], tx1w[(s, 1)], tA_sqe[(s, 1)],
                           tA_end[(s, 1)]]
                    if s > 0:
                        wW2.append(tB_mm2[(s - 1, 1, 2, 1)])
                else:
                    wW2 = [tB_mm2[(s,) + passes[pi - 1]]]
                if j > 0:
                    wW2.append(b.hold[("g2", s, j - 1)])
                w2src = dw2m if j == 0 else dws2[j - 1]
                for hc in range(NHC):
                    b.dma("gpsimd", "gW2", W2h(hc, 0, 1024),
                          w2src[h * 2048 + hc * 128:h * 2048 + (hc + 1) * 128, :],
                          wW2 if hc == 0 else ())
                tW2 = [fence("gpsimd", "gW2")] * NHC

                for dc in range(NDC):
                    bk = [4 + (dc % 2) * 2 + tt for tt in range(NTT)]
                    for hc in range(NHC):
                        for tt in range(NTT):
                            mm(pb[bk[tt]][:, :], W2h(hc, dc * 128, (dc + 1) * 128),
                               Tt(hc, tt * 512, (tt + 1) * 512),
                               hc == 0, hc == NHC - 1,
                               [tW2[hc], t_ev1[hc]], bank=bk[tt])
                    for tt in range(NTT):
                        wE = [tOA.get((dc, tt))]
                        if j == 0 and h == 0:
                            if p == 0 and dc == 0 and tt == 0:
                                wE.append(tA_end[(s, 1)])
                            if p == 1 and dc == 0 and tt == 0:
                                wE += [tB5v[(s, 0)],
                                       ("sxS", b.cnt["sxS"])]
                            tOA[(dc, tt)] = rb(
                               "scalar",
                               lambda E, dc=dc, tt=tt: E.activation(
                                   out=OA(dc, tt * 512, (tt + 1) * 512),
                                   in_=pb[4 + (dc % 2) * 2 + tt][:, :],
                                   func=AF.Identity, scale=abc[:, 0:1],
                                   bias=B2t[:, dc:dc + 1]),
                               bk[tt], waits=wE)
                        else:
                            sc = abc[:, 0:1] if j == 0 else wkbc[:, j - 1:j]
                            tOA[(dc, tt)] = rb(
                               "vector",
                               lambda E, dc=dc, tt=tt, sc=sc:
                               E.scalar_tensor_tensor(
                                   out=OA(dc, tt * 512, (tt + 1) * 512),
                                   in0=pb[4 + (dc % 2) * 2 + tt][:, :],
                                   scalar=sc,
                                   in1=OA(dc, tt * 512, (tt + 1) * 512),
                                   op0=ALU.mult, op1=ALU.add),
                               bk[tt], waits=wE)
                tB_mm2[(s, p, j, h)] = ("tv", b.cnt["tv"])

                if j == 2 and h == 1:
                    post_ln(s, p)

        def post_ln(s, p):
            c0 = p * ST
            wF2 = [tB_mm1[(s, p, 2, 1)]]
            wF3 = [tB_mm2[(s, p, 2, 1)]]
            t_obf, t_osq = [], []
            for dc in range(NDC):
                woa = [tOA.get((dc, tt)) for tt in range(NTT)]
                t_obf.append(vop(lambda E, dc=dc: E.tensor_copy(
                    out=F2c(dc), in_=OA(dc)),
                    (wF2 if dc == 0 else []) + woa, inc="vv"))
                t_osq.append(sop(lambda E, dc=dc: E.activation(
                    out=F3b(dc), in_=OA(dc), func=AF.Square),
                    (wF3 if dc == 0 else []) + woa, inc="sv"))
            for dc in range(NDC):
                for tt in range(NTT):
                    mm(pb[tt][0:1, :], onesb[:, :], F2c(dc, tt * 512, (tt + 1) * 512),
                       dc == 0, dc == NDC - 1, [t_obf[dc]], bank=tt)
                    mm(pb[2 + tt][0:1, :], onesb[:, :], F3b(dc, tt * 512, (tt + 1) * 512),
                       dc == 0, dc == NDC - 1, [t_osq[dc]], bank=2 + tt)
            for tt in range(NTT):
                rb("vector", lambda E, tt=tt: E.tensor_copy(
                    out=sbst[0:1, tt * 512:(tt + 1) * 512], in_=pb[tt][0:1, :]), tt)
                rb("vector", lambda E, tt=tt: E.tensor_copy(
                    out=sbst[0:1, 1024 + tt * 512:1024 + (tt + 1) * 512],
                    in_=pb[2 + tt][0:1, :]), 2 + tt)
            t_se = vop(lambda E: E.engine_nop(), inc="vv")
            tB5mm[(s, p)] = ("tv", b.cnt["tv"])
            for tt in range(NTT):
                mm(pb[4 + tt][:, :], selT[0:4, 0:128],
                   sbst[0:4, tt * 512:(tt + 1) * 512], True, True, [t_se],
                   bank=4 + tt)
                mm(pb[6 + tt][:, :], selT[0:4, 0:128],
                   sbst[0:4, 1024 + tt * 512:1024 + (tt + 1) * 512], True, True,
                   [t_se], bank=6 + tt)
            tB5mm[(s, p)] = ("tv", b.cnt["tv"])
            for tt in range(NTT):
                rb("vector", lambda E, tt=tt: E.tensor_scalar(
                    out=BCA[:, tt * 512:(tt + 1) * 512], in0=pb[4 + tt][:, :],
                    scalar1=1.0 / D, scalar2=None, op0=ALU.mult), 4 + tt,
                   waits=[("sv", b.cnt["sv"])] if tt == 0 else ())
            vop(lambda E: E.tensor_mul(out=T12[:, :], in0=BCA[:, :], in1=BCA[:, :]))
            for tt in range(NTT):
                rb("vector", lambda E, tt=tt: E.scalar_tensor_tensor(
                    out=BCB[:, tt * 512:(tt + 1) * 512], in0=pb[6 + tt][:, :],
                    scalar=1.0 / D, in1=T12[:, tt * 512:(tt + 1) * 512],
                    op0=ALU.mult, op1=ALU.subtract), 6 + tt)
            tv4 = ("vv", b.cnt["vv"])
            sop(lambda E: E.activation(out=BCB[:, :], in_=BCB[:, :],
                                       func=AF.Ln, bias=epsb[:, 0:1]), [tv4])
            t_r5 = sop(lambda E: E.activation(out=BCB[:, :], in_=BCB[:, :],
                                              func=AF.Exp, scale=-0.5), inc="sv")
            tB5s[(s, p)] = t_r5
            for dc in range(NDC):
                b.dma("sync", "sxR", F3f[:, dc * 1024:(dc + 1) * 1024],
                      dx1[s, dc * 128:(dc + 1) * 128, c0:c0 + ST],
                      [tB5mm[(s, p)]] if dc == 0 else ())
            trd = fence("sync", "sxR")
            t_apl = [None] * NDC
            for dc in range(NDC):
                vop(lambda E, dc=dc: E.tensor_sub(out=OA(dc), in0=OA(dc),
                                                  in1=BCA),
                    ([t_r5] if dc == 0 else []) + [tOA.get((dc, tt))
                                                  for tt in range(NTT)])
                vop(lambda E, dc=dc: E.tensor_mul(out=OA(dc), in0=OA(dc),
                                                  in1=BCB))
                t_apl[dc] = vop(lambda E, dc=dc: E.tensor_add(
                    out=OA(dc), in0=OA(dc),
                    in1=F3f[:, dc * 1024:(dc + 1) * 1024]), [trd], inc="vv")
                toutw[s] = b.dma("sync", "sxS",
                                 dout[s, dc * 128:(dc + 1) * 128, c0:c0 + ST],
                                 OA(dc), [t_apl[dc]])
            tB5v[(s, p)] = vop(lambda E: E.engine_nop(), inc="vv")

        for s in range(NSEQ):
            if s > 0:
                b.seg()
            for p in range(NST):
                phase_a_stripe(s, p)
            if mode == "A":
                for p in range(NST):
                    for dc in range(NDC):
                        b.dma("sync", "sxS",
                              dout[s, dc * 128:(dc + 1) * 128,
                                   p * ST:(p + 1) * ST],
                              xs(dc), [tA_end[(s, p)]])
                continue
            t_b2 = gate(s)
            if mode.startswith("AG"):
                for p in range(NST):
                    for dc in range(NDC):
                        b.dma("sync", "sxS",
                              dout[s, dc * 128:(dc + 1) * 128,
                                   p * ST:(p + 1) * ST],
                              xs(dc), [t_b2])
                continue
            phase_b(s, t_b2)

        b.do("gpsimd", lambda E: E.nop(), [("sxS", b.cnt["sxS"]),
                                           ("sxL", b.cnt["sxL"]),
                                           ("sxR", b.cnt["sxR"]),
                                           ("sxB", b.cnt["sxB"]),
                                           ("gW1", b.cnt["gW1"]),
                                           ("gW2", b.cnt["gW2"]),
                                           ("gB", b.cnt["gB"])])

        with nc.Block() as blk:
            b.emit(blk)

    return nc


_PROG = None


def _get_prog():
    global _PROG
    if _PROG is None:
        _PROG = build_program()
    return _PROG


def kernel(L0_attn_x, L0_attn_W, L0_attn_b, L0_attn_gamma, L0_attn_beta,
           L0_ffn_alpha, L0_ffn_w_gate, L0_ffn_b_gate, L0_ffn_W1m, L0_ffn_b1m,
           L0_ffn_W2m, L0_ffn_b2m, L0_ffn_W1e, L0_ffn_b1e, L0_ffn_W2e,
           L0_ffn_b2e):
    f = np.float32
    x = np.asarray(L0_attn_x, f)
    aw = np.ascontiguousarray(np.asarray(L0_attn_W, f))
    ab = np.ascontiguousarray(np.asarray(L0_attn_b, f).reshape(16, 128).T)
    gam = np.ascontiguousarray(np.asarray(L0_attn_gamma, f).reshape(8, 128).T)
    bet = np.ascontiguousarray(np.asarray(L0_attn_beta, f).reshape(8, 128).T)
    wg = np.ascontiguousarray(np.asarray(L0_ffn_w_gate, f))
    bg = np.ascontiguousarray(np.asarray(L0_ffn_b_gate, f).reshape(1, 8))
    w1m = np.ascontiguousarray(np.asarray(L0_ffn_W1m, f))
    b1m = np.ascontiguousarray(np.asarray(L0_ffn_b1m, f).reshape(32, 128).T)
    w2m = np.ascontiguousarray(np.asarray(L0_ffn_W2m, f))
    b2m = np.ascontiguousarray(np.asarray(L0_ffn_b2m, f).reshape(8, 128).T)
    w1e = np.ascontiguousarray(np.asarray(L0_ffn_W1e, f))
    b1t = np.ascontiguousarray(
        np.asarray(L0_ffn_b1e, f).reshape(8, 32, 128).transpose(0, 2, 1))
    w2e = np.ascontiguousarray(np.asarray(L0_ffn_W2e, f))
    b2t = np.ascontiguousarray(
        np.asarray(L0_ffn_b2e, f).reshape(8, 8, 128).transpose(0, 2, 1))
    al = np.asarray(L0_ffn_alpha, f).reshape(1, 1)
    io8 = np.arange(8, dtype=f).reshape(1, 8)
    sel = np.zeros((4, 512), f)
    for r in range(4):
        sel[r, r * 128:(r + 1) * 128] = 1.0

    nc = _get_prog()
    shared = dict(aw=aw, ab=ab, gam=gam, bet=bet, wg=wg, bg=bg, w1m=w1m,
                  b1m=b1m, w2m=w2m, b2m=b2m, w1e=w1e, b1t=b1t, w2e=w2e,
                  b2t=b2t, al=al, io8=io8, sel=sel)
    in_maps = []
    for c in range(8):
        xc = np.ascontiguousarray(
            x[c * NSEQ:(c + 1) * NSEQ].transpose(0, 2, 1))
        in_maps.append(dict(x=xc, **shared))
    res = run_bass_kernel_spmd(nc, in_maps, core_ids=list(range(8)))
    outs = [np.asarray(r["out"]).transpose(0, 2, 1) for r in res.results]
    return np.ascontiguousarray(np.concatenate(outs, axis=0))
